# version 01
# baseline (speedup 1.0000x reference)
import math
import numpy as np
import ml_dtypes
import concourse.bass as bass
import concourse.mybir as mybir
from concourse.bass_utils import run_bass_kernel_spmd

F32 = mybir.dt.float32
BF16 = mybir.dt.bfloat16
AF = mybir.ActivationFunctionType
ALU = mybir.AluOpType
AX = mybir.AxisListType
BFD = ml_dtypes.bfloat16

D = 2048
T = 8192
NCORES = 8
TLOC = 1024
NT = 8
H = 16
HKV = 4
MB = 64
NBLK = 112
LW = 1024
KEYS = NBLK + LW
E = 8192
ND = D // 128
EC = E // 128
EBLK = 8
NEB = EC // EBLK
MAGIC = 12582912.0
EPS_LN = 1e-5
EPS_Q = 1e-8
ISQ32 = 1.0 / math.sqrt(32.0)

KWJ = [16 * (j + 1) for j in range(7)] + [KEYS]
MOFF = [0]
for _j in range(8):
    MOFF.append(MOFF[_j] + KWJ[_j])
MTOT = MOFF[8]

NQCH = 32
NSCH = 32

_PHASE_LIMIT = None
_TAPS = []


def _cdiv(a, b):
    return (a + b - 1) // b


class _Sem:
    def __init__(self, h):
        self.h = h
        self.c = 0

    def inc(self, instr, n=1):
        instr.then_inc(self.h, n)
        self.c += n
        return self.c


def _build():
    nc = bass.Bass("TRN2", target_bir_lowering=False, debug=False)

    ap_x = nc.dram_tensor("x", [TLOC, D], F32, kind="ExternalInput").ap()
    ap_wq = nc.dram_tensor("wq", [8 * ND * 128, 128], F32, kind="ExternalInput").ap()
    ap_wkv = nc.dram_tensor("wkv", [4 * ND * 128, 128], F32, kind="ExternalInput").ap()
    ap_wo = nc.dram_tensor("wo", [4 * 8 * 128, 512], F32, kind="ExternalInput").ap()
    ap_wup = nc.dram_tensor("wup", [4096, 512], F32, kind="ExternalInput").ap()
    ap_wgate = nc.dram_tensor("wgate", [4096, 512], F32, kind="ExternalInput").ap()
    ap_wdown = nc.dram_tensor("wdown", [4096, 512], F32, kind="ExternalInput").ap()
    ap_ls = nc.dram_tensor("ls", [1, 16], F32, kind="ExternalInput").ap()
    ap_mask = nc.dram_tensor("mask", [128, MTOT], BF16, kind="ExternalInput").ap()
    ap_cos = nc.dram_tensor("cosq", [128, TLOC], BF16, kind="ExternalInput").ap()
    ap_sin = nc.dram_tensor("sinq", [128, TLOC], BF16, kind="ExternalInput").ap()
    ap_pperm = nc.dram_tensor("pperm", [128, 128], BF16, kind="ExternalInput").ap()
    ap_ident = nc.dram_tensor("ident", [128, 128], BF16, kind="ExternalInput").ap()
    ap_pgp = nc.dram_tensor("pgp", [128, 128], BF16, kind="ExternalInput").ap()
    ap_jmat = nc.dram_tensor("jmat", [128, 384], BF16, kind="ExternalInput").ap()
    ap_f01 = nc.dram_tensor("f01", [1, 128], BF16, kind="ExternalInput").ap()
    ap_rdup = nc.dram_tensor("rdup", [128, 512], BF16, kind="ExternalInput").ap()
    ap_nf = nc.dram_tensor("nf", [128, 128], BF16, kind="ExternalInput").ap()
    ap_fl63 = nc.dram_tensor("fl63", [128, 1], F32, kind="ExternalInput").ap()
    ap_cone = nc.dram_tensor("c_one1", [128, 1], F32, kind="ExternalInput").ap()
    ap_ceps = nc.dram_tensor("c_eps", [128, 1], F32, kind="ExternalInput").ap()
    ap_cv136 = nc.dram_tensor("c_v136", [128, 1], BF16, kind="ExternalInput").ap()
    ap_cbcs = nc.dram_tensor("c_bcs", [128, 8], F32, kind="ExternalInput").ap()
    ap_conerow = nc.dram_tensor("c_onerow", [1, 128], F32, kind="ExternalInput").ap()
    ap_zf64 = nc.dram_tensor("c_zf64", [128, 64], F32, kind="ExternalInput").ap()
    ap_zf4 = nc.dram_tensor("c_zf4", [128, 4], F32, kind="ExternalInput").ap()
    ap_zrow = nc.dram_tensor("c_zrow", [1, 8], F32, kind="ExternalInput").ap()
    ap_zbf = nc.dram_tensor("c_zbf", [128, 512], BF16, kind="ExternalInput").ap()
    ap_zmvd = nc.dram_tensor("c_zmvd", [1, 1024], BF16, kind="ExternalInput").ap()
    ap_zmv = nc.dram_tensor("c_zmv", [128, 2], BF16, kind="ExternalInput").ap()
    ap_out = nc.dram_tensor("xout", [TLOC, D], F32, kind="ExternalOutput").ap()

    kv_in = nc.dram_tensor("kv_in", [142, 512], BF16)
    kv_out = nc.dram_tensor("kv_out", [KEYS, 512], BF16)
    sc_in = nc.dram_tensor("sc_in", [1, 8], F32)
    sc_out = nc.dram_tensor("sc_out", [1, 8], F32)
    q_in = {m: nc.dram_tensor(f"q{m}_in", [4096, 512], BF16)
            for m in ("up", "gate", "down")}
    q_out = {m: nc.dram_tensor(f"q{m}_out", [32768, 512], BF16)
             for m in ("up", "gate", "down")}
    RG = [list(range(NCORES))]

    tap_aps = {}
    for (tname, _k, tp, tcn) in _TAPS:
        tap_aps[tname] = nc.dram_tensor("tap_" + tname, [tp, tcn], F32,
                                        kind="ExternalOutput").ap()

    ctx = []

    def sb(name, shape, dt):
        t = nc.sbuf_tensor(name, shape, dt)
        ctx.append(t)
        return t.__enter__()

    def pst(name, shape, dt):
        t = nc.psum_tensor(name, shape, dt)
        ctx.append(t)
        return t.__enter__()

    def mksem(name):
        s = nc.semaphore(name)
        ctx.append(s)
        return _Sem(s.__enter__())

    XT = [sb(f"xt{t}", [128, D], F32) for t in range(NT)]
    LF = sb("lf", [128, ND * TLOC], BF16)
    HB = sb("hb", [128, 8192], BF16)
    WST = sb("wst", [128, 1024], F32)
    QBF = sb("qbf_sb", [128, 2048], BF16)
    ARENA = sb("arena", [128, 23040], BF16)
    KCH = [sb(f"kch{k}", [64, KEYS], BF16) for k in range(HKV)]
    VTS = sb("vts", [128, 9 * 256], BF16)
    PRB = sb("prb", [128, KEYS], BF16)
    PT = sb("pt", [128, 9 * 128], BF16)
    MASK = sb("mask_sb", [128, MTOT], BF16)
    COS = sb("cos_sb", [128, TLOC], BF16)
    SIN = sb("sin_sb", [128, TLOC], BF16)
    PPERM = sb("pperm_sb", [128, 128], BF16)
    IDENT = sb("ident_sb", [128, 128], BF16)
    PGP = sb("pgp_sb", [128, 128], BF16)
    JMAT = sb("jmat_sb", [128, 384], BF16)
    F01 = sb("f01_sb", [1, 128], BF16)
    RDUP = sb("rdup_sb", [128, 512], BF16)
    NF = sb("nf_sb", [128, 128], BF16)
    FL63 = sb("fl63_sb", [128, 1], F32)
    LSR = sb("lsr", [1, 16], F32)

    SUMX = sb("sumx", [128, NT], F32)
    SQS = sb("sqs", [128, NT], F32)
    MEANB = sb("meanb", [128, NT], F32)
    VARB = sb("varb", [128, NT], F32)
    SDB = sb("sdb", [128, NT], F32)
    RIB = sb("rib", [128, NT], F32)
    TMP1 = sb("tmp1", [128, 1], F32)
    TMP8 = sb("tmp8", [128, NT], F32)
    TMPS = sb("tmps", [128, 1], F32)
    POOLT = sb("poolt", [128, 64], F32)
    PP = sb("pp", [14, 512], BF16)
    PW = sb("pw", [128, 512], BF16)
    ABSC = sb("absc", [128, 96], F32)
    ABSA = sb("absa", [128, 4], F32)
    ONE1 = sb("one1", [128, 1], F32)
    EPSV = sb("epsv", [128, 1], F32)
    ONEROW = sb("onerow", [1, 128], F32)
    SCRO = sb("scro", [1, 8], F32)
    SCI = sb("sci", [1, 8], F32)
    SCW = sb("scw", [1, 8], F32)
    BC = sb("bc", [128, 8], F32)
    ALS = sb("als", [1, 16], F32)
    BCS = sb("bcs", [128, 8], F32)
    BCR = sb("bcr", [128, 8], F32)
    AB = sb("ab", [128, 16], F32)
    V136 = sb("v136", [128, 1], BF16)
    MV = sb("mv", [128, 2], BF16)
    MVR = [sb(f"mvr{i}", [1, 128], BF16) for i in range(2)]
    MVD = sb("mvd", [1, 1024], BF16)
    NM = sb("nm", [128, 3], F32)
    NMB = sb("nmb", [128, 1], F32)
    SEP = sb("sep", [128, 3], F32)
    RC = sb("rc", [128, 1], F32)
    QHS = [sb(f"qh{i}", [64, TLOC], BF16) for i in range(4)]

    PF = [pst(f"pf{i}", [128, 512], F32) for i in range(8)]

    sX = mksem("sX")
    sG = mksem("sG")
    sCON = mksem("sCON")
    sSCL = [mksem("sSCL0"), mksem("sSCL1")]
    sQI = [mksem("sQI0"), mksem("sQI1")]
    sQO = [mksem("sQO0"), mksem("sQO1")]
    sW = [mksem("sW0"), mksem("sW1")]
    sWOD = mksem("sWOD")
    sPC = mksem("sPC")
    sSC2 = mksem("sSC2")
    sFU = [mksem("sFU0"), mksem("sFU1")]
    sFG = [mksem("sFG0"), mksem("sFG1")]
    sFD = mksem("sFD")
    sTAP = mksem("sTAP")
    sCC = mksem("sCC")
    sTE = mksem("sTE")
    sVE = mksem("sVE")
    sSE = mksem("sSE")
    sST = mksem("sST")

    blk = nc.Block()
    ctx.append(blk)
    B = blk.__enter__()

    A_QW, A_G, A_AF, A_WO = 0, 4096, 10752, 18944
    A_FU, A_FG, A_FD = 0, 4096, 6144


    def kvt(i):
        return ARENA[:, A_AF + 1024 * i: A_AF + 1024 * (i + 1)]

    def aft(g):
        return ARENA[:, A_AF + 1024 * g: A_AF + 1024 * (g + 1)]

    def qwslab(s):
        return ARENA[:, A_QW + 2048 * s: A_QW + 2048 * (s + 1)]

    def gwt(c):
        return ARENA[:, A_G + 512 * c: A_G + 512 * (c + 1)]

    GPT = ARENA[:, A_G + 4096: A_G + 4608]
    WOS = ARENA[:, A_WO: A_WO + 4096]

    def fus(s):
        return ARENA[:, 2048 * s: 2048 * (s + 1)]

    def fgs(s):
        return ARENA[:, 4096 + 2048 * s: 4096 + 2048 * (s + 1)]

    def fds(i):
        return ARENA[:, 8192 + 2048 * i: 8192 + 2048 * (i + 1)]

    def fdsl(i):
        return QBF[:, :] if i == 7 else fds(i)

    LNT = HB[:, 2048:4096]
    SQD = HB[:, 0:2048]
    tiles = {}

    def qbf(s):
        return QBF[:, 512 * s:512 * (s + 1)]

    @B.sync
    def _(s):
        for t in range(NT):
            sX.inc(s.dma_start(out=XT[t][:, :],
                               in_=ap_x[128 * t:128 * (t + 1), :]), 16)
    xld_all = sX.c

    @B.scalar
    def _(s):
        for dst, srcin in [(MASK, ap_mask), (COS, ap_cos), (SIN, ap_sin),
                           (PPERM, ap_pperm), (IDENT, ap_ident), (PGP, ap_pgp),
                           (JMAT, ap_jmat), (NF, ap_nf), (F01, ap_f01),
                           (FL63, ap_fl63), (LSR, ap_ls), (RDUP, ap_rdup),
                           (ONE1, ap_cone), (EPSV, ap_ceps), (V136, ap_cv136),
                           (BCS, ap_cbcs), (ONEROW, ap_conerow), (POOLT, ap_zf64),
                           (ABSA, ap_zf4), (SCRO, ap_zrow), (MVD, ap_zmvd),
                           (MV, ap_zmv)]:
            sCON.inc(s.dma_start(out=dst[:, :], in_=srcin), 16)
        sCON.inc(s.dma_start(out=ARENA[:, A_G + 4096:A_G + 4608], in_=ap_zbf), 16)
    consts_ld = sCON.c

    def emit_wslab(src_rows, slot, waits):
        @B.gpsimd
        def _(gp):
            for (sh, val) in waits:
                gp.wait_ge(sh.h, val)
            sW[slot].inc(gp.dma_start(
                out=qwslab(slot).rearrange("p (dc e) -> p dc e", dc=ND, e=128),
                in_=src_rows.rearrange("(dc d) e -> d dc e", dc=ND, d=128)), 16)
        return (sW[slot], sW[slot].c)

    wkv_ld = {}
    wkv_ld[0] = emit_wslab(ap_wkv[0:ND * 128, :], 0, [])
    wkv_ld[1] = emit_wslab(ap_wkv[ND * 128:2 * ND * 128, :], 1, [])

    memset_done = None
    tiles["_"] = LF
    if _PHASE_LIMIT == 0:
        return _close(_finish(nc, B, tiles, tap_aps, locals()), ctx)

    def emit_ln(is_ln2, wait_v=None):
        @B.vector
        def _(v):
            if not is_ln2:
                v.wait_ge(sX.h, xld_all)
                v.wait_ge(sCON.h, consts_ld)
            for t in range(NT):
                ins = v.tensor_reduce(SUMX[:, t:t + 1], XT[t][:, :], AX.X, ALU.add)
            ins
            sVE.inc(v.tensor_copy(TMP1[:, :], ONE1[:, :]))
        sums_done = sVE.c

        @B.scalar
        def _(s):
            if wait_v is not None:
                s.wait_ge(sVE.h, wait_v)
            if not is_ln2:
                s.wait_ge(sX.h, xld_all)
                s.wait_ge(sCON.h, consts_ld)
            for t in range(NT):
                ins = s.activation(SQD[:, :], XT[t][:, :], AF.Square,
                                   accum_out=SQS[:, t:t + 1])
            ins
            sSE.inc(s.copy(TMPS[:, :], ONE1[:, :]))
        sq_done = sSE.c

        @B.vector
        def _(v):
            v.wait_ge(sSE.h, sq_done)
            v.tensor_scalar_mul(MEANB[:, :], SUMX[:, :], 1.0 / D)
            v.tensor_scalar_mul(VARB[:, :], SQS[:, :], 1.0 / D)
            v.tensor_tensor(TMP8[:, :], MEANB[:, :], MEANB[:, :], ALU.mult)
            v.tensor_copy(TMP1[:, :], ONE1[:, :])
            v.tensor_copy(TMP1[:, :], ONE1[:, :])
            v.tensor_tensor(VARB[:, :], VARB[:, :], TMP8[:, :], ALU.subtract)
            sVE.inc(v.tensor_copy(TMP1[:, :], ONE1[:, :]))
        var_done = sVE.c

        @B.scalar
        def _(s):
            s.wait_ge(sVE.h, var_done)
            for t in range(NT):
                ins = s.activation(SDB[:, t:t + 1], VARB[:, t:t + 1], AF.Sqrt,
                                   bias=EPSV[:, :])
            ins
            sSE.inc(s.copy(TMPS[:, :], ONE1[:, :]))
        sd_done = sSE.c

        @B.vector
        def _(v):
            v.wait_ge(sSE.h, sd_done)
            v.reciprocal(RIB[:, :], SDB[:, :])
            v.tensor_copy(TMP1[:, :], ONE1[:, :])
            v.tensor_copy(TMP1[:, :], ONE1[:, :])
            sVE.inc(v.tensor_copy(TMP1[:, :], ONE1[:, :]))

        last = None
        for t in range(NT):
            @B.vector
            def _(v, t=t):
                v.tensor_scalar(LNT[:, :], XT[t][:, :],
                                        MEANB[:, t:t + 1], RIB[:, t:t + 1],
                                        ALU.subtract, ALU.mult)
                sVE.inc(v.tensor_copy(TMP1[:, :], ONE1[:, :]))
            apply_done = sVE.c

            for half in range(2):
                @B.tensor
                def _(te, t=t, half=half):
                    te.wait_ge(sVE.h, apply_done if half == 0 else sVE.c)
                    te.wait_ge(sCON.h, consts_ld)
                    for k8 in range(8):
                        dc = 8 * half + k8
                        ins = te.matmul(PF[k8][0:128, 0:128],
                                  LNT[:, 128 * dc:128 * (dc + 1)], IDENT[:, :],
                                  start=True, stop=True)
                    sTE.inc(ins)
                tr_done = sTE.c

                @B.vector
                def _(v, t=t, half=half):
                    v.wait_ge(sTE.h, tr_done)
                    for k8 in range(8):
                        dc = 8 * half + k8
                        ins = v.tensor_copy(
                            LF[:, 1024 * dc + 128 * t:1024 * dc + 128 * (t + 1)],
                            PF[k8][0:128, 0:128])
                    ins
                    sVE.inc(v.tensor_copy(TMP1[:, :], ONE1[:, :]))
            last = sVE.c
        return last

    ln1_done = emit_ln(False)
    tiles["lf"] = LF
    if _PHASE_LIMIT == 1:
        return _close(_finish(nc, B, tiles, tap_aps, locals()), ctx)

    kv_evac = []
    kv_mm = []
    for cc in range(4):
        slot = cc % 2

        @B.tensor
        def _(te, cc=cc, slot=slot):
            te.wait_ge(wkv_ld[cc][0].h, wkv_ld[cc][1])
            if cc == 0:
                te.wait_ge(sVE.h, ln1_done)
            else:
                te.wait_ge(sVE.h, kv_evac[cc - 1])
            for th in range(2):
                for dc in range(ND):
                    ins = te.matmul(PF[th][:, :],
                              qwslab(slot)[:, 128 * dc:128 * (dc + 1)],
                              LF[:, 1024 * dc + 512 * th:1024 * dc + 512 * (th + 1)],
                              start=(dc == 0), stop=(dc == ND - 1))
            sTE.inc(ins)
        kv_mm.append(sTE.c)

        @B.vector
        def _(v, cc=cc):
            v.wait_ge(sTE.h, kv_mm[cc])
            for th in range(2):
                ins = v.tensor_copy(kvt(cc)[:, 512 * th:512 * (th + 1)], PF[th][:, :])
            ins
            sVE.inc(v.tensor_copy(TMP1[:, :], ONE1[:, :]))
        kv_evac.append(sVE.c)

        if cc < 2:
            wkv_ld[cc + 2] = emit_wslab(
                ap_wkv[(cc + 2) * ND * 128:(cc + 3) * ND * 128, :], slot,
                [(sTE, kv_mm[cc])])

    if _PHASE_LIMIT == 20:
        return _close(_finish(nc, B, tiles, tap_aps, locals()), ctx)

    rope_k_done = []
    for i in range(2):
        for th in range(2):
            @B.tensor
            def _(te, i=i, th=th):
                te.wait_ge(sVE.h, kv_evac[i])
                if rope_k_done:
                    te.wait_ge(sVE.h, rope_k_done[-1])
                ins = te.matmul(PF[2][:, :], PPERM[:, :],
                          kvt(i)[:, 512 * th:512 * (th + 1)], start=True, stop=True)
                sTE.inc(ins)
            rot_done = sTE.c

            @B.vector
            def _(v, i=i, th=th):
                v.wait_ge(sTE.h, rot_done)
                sl = slice(512 * th, 512 * (th + 1))
                v.tensor_tensor(kvt(i)[:, sl], kvt(i)[:, sl], COS[:, sl], ALU.mult)
                v.tensor_tensor(PRB[:, 0:512], PF[2][:, :], SIN[:, sl], ALU.mult)
                v.tensor_copy(TMP1[:, :], ONE1[:, :])
                v.tensor_tensor(kvt(i)[:, sl], kvt(i)[:, sl],
                                        PRB[:, 0:512], ALU.add)
                sVE.inc(v.tensor_copy(TMP1[:, :], ONE1[:, :]))
            rope_k_done.append(sVE.c)

    if _PHASE_LIMIT == 21:
        return _close(_finish(nc, B, tiles, tap_aps, locals()), ctx)

    @B.scalar
    def _(s):
        s.wait_ge(sVE.h, max(rope_k_done[-1], kv_evac[3]))
        for i in range(4):
            for j in range(7):
                for s2 in range(2):
                    col = 16 * i + 2 * j + s2
                    ins = s.activation(
                        SQD[:, 0:64],
                        kvt(i)[:, 128 * j + 64 * s2:128 * j + 64 * (s2 + 1)],
                        AF.Copy, accum_out=POOLT[:, col:col + 1])
        ins
        sSE.inc(s.copy(TMPS[:, :], ONE1[:, :]))
    pool_done_s = sSE.c

    POOLB = sb("poolb", [128, 64], BF16)

    @B.vector
    def _(v):
        v.wait_ge(sSE.h, pool_done_s)
        v.tensor_copy(POOLB[:, :], POOLT[:, :])
        sVE.inc(v.tensor_copy(TMP1[:, :], ONE1[:, :]))
    poolb_done = sVE.c

    @B.tensor
    def _(te):
        te.wait_ge(sVE.h, poolb_done)
        for i in range(4):
            ins = te.matmul(PF[i][0:14, 0:128], POOLB[:, 16 * i:16 * i + 14],
                      IDENT[:, :], start=True, stop=True)
        for i in range(4):
            ins = te.matmul(PF[4 + i][0:128, 0:128], kvt(i)[:, 896:1024],
                      IDENT[:, :], start=True, stop=True)
        sTE.inc(ins)
    piece_tr = sTE.c

    @B.vector
    def _(v):
        v.wait_ge(sTE.h, piece_tr)
        for i in range(4):
            v.tensor_scalar_mul(PP[0:14, 128 * i:128 * (i + 1)],
                                PF[i][0:14, 0:128], 1.0 / MB)
        for i in range(4):
            ins = v.tensor_copy(PW[:, 128 * i:128 * (i + 1)],
                                PF[4 + i][0:128, 0:128])
        ins
        sVE.inc(v.tensor_copy(TMP1[:, :], ONE1[:, :]))
    piece_done = sVE.c

    if _PHASE_LIMIT == 22:
        return _close(_finish(nc, B, tiles, tap_aps, locals()), ctx)

    @B.gpsimd
    def _(gp):
        gp.wait_ge(sVE.h, piece_done)
        sPC.inc(gp.dma_start(out=kv_in.ap()[0:14, :], in_=PP[:, :]), 16)
        sPC.inc(gp.dma_start(out=kv_in.ap()[14:142, :], in_=PW[:, :]), 16)
        gp.wait_ge(sPC.h, sPC.c)
        sCC.inc(gp.collective_compute(
            "AllGather", ALU.bypass, replica_groups=RG,
            ins=[kv_in.ap()], outs=[kv_out.ap()]))
    cc_kv = sCC.c
    if _PHASE_LIMIT == 2:
        return _close(_finish(nc, B, tiles, tap_aps, locals()), ctx)

    sc_dma = []
    sc_abs = []
    srcs = [ap_wup, ap_wgate, ap_wdown]
    for idx in range(3 * NSCH + 2):
        if idx >= 2:
            jdx = idx - 2
            slot = jdx % 2

            @B.scalar
            def _(s, jdx=jdx, slot=slot):
                s.wait_ge(sSCL[slot].h, sc_dma[jdx])
                s.activation(SQD[:, 0:512],
                                     WST[:, 512 * slot:512 * (slot + 1)],
                                     AF.Abs, accum_out=ABSC[:, jdx:jdx + 1])
                sSE.inc(s.copy(TMPS[:, :], ONE1[:, :]))
            sc_abs.append(sSE.c)
        if idx < 3 * NSCH:
            m, ch = divmod(idx, NSCH)
            slot = idx % 2

            @B.scalar
            def _(s, m=m, ch=ch, slot=slot, idx=idx):
                if idx >= 2:
                    s.wait_ge(sSE.h, sc_abs[idx - 2])
                sSCL[slot].inc(s.dma_start(out=WST[:, 512 * slot:512 * (slot + 1)],
                                    in_=srcs[m][128 * ch:128 * (ch + 1), :]), 16)
            sc_dma.append(sSCL[slot].c)

    @B.vector
    def _(v):
        v.wait_ge(sSE.h, sc_abs[-1])
        for m in range(3):
            ins = v.tensor_reduce(ABSA[:, m:m + 1],
                                  ABSC[:, NSCH * m:NSCH * (m + 1)],
                                  AX.X, ALU.add)
        ins
        sVE.inc(v.tensor_copy(TMP1[:, :], ONE1[:, :]))
    absa_done = sVE.c

    @B.tensor
    def _(te):
        te.wait_ge(sVE.h, absa_done)
        ins = te.matmul(PF[5][0:1, 0:4], ONE1[:, :], ABSA[:, :], start=True, stop=True)
        sTE.inc(ins)
    scred_done = sTE.c

    @B.vector
    def _(v):
        v.wait_ge(sTE.h, scred_done)
        v.tensor_copy(SCRO[0:1, 0:4], PF[5][0:1, 0:4])
        sVE.inc(v.tensor_copy(TMP1[:, :], ONE1[:, :]))
    scro_done = sVE.c

    @B.gpsimd
    def _(gp):
        gp.wait_ge(sVE.h, scro_done)
        sSC2.inc(gp.dma_start(out=sc_in.ap(), in_=SCRO[:, :]), 16)
        gp.wait_ge(sSC2.h, sSC2.c)
        sCC.inc(gp.collective_compute(
            "AllReduce", ALU.add, replica_groups=RG,
            ins=[sc_in.ap()], outs=[sc_out.ap()]))
        gp.wait_ge(sCC.h, sCC.c)
        sSC2.inc(gp.dma_start(out=SCI[:, :], in_=sc_out.ap()), 16)
    sci_ld = sSC2.c
    cc_scale = sCC.c

    @B.tensor
    def _(te):
        te.wait_ge(sSC2.h, sci_ld)
        ins = te.matmul(PF[5][0:128, 0:8], ONEROW[:, :], SCI[:, :],
                        start=True, stop=True)
        sTE.inc(ins)
    scib_mm = sTE.c

    @B.vector
    def _(v):
        v.wait_ge(sTE.h, scib_mm)
        v.tensor_copy(TMP1[:, :], ONE1[:, :])
        v.tensor_scalar(BCS[:, 0:4], PF[5][0:128, 0:4], 1.0 / 16777216.0, EPS_Q,
                        ALU.mult, ALU.add)
        v.tensor_copy(TMP1[:, :], ONE1[:, :])
        v.tensor_copy(BC[:, 4:8], BCS[:, 0:4])
        v.reciprocal(BCR[:, :], BCS[:, :])
        v.tensor_copy(TMP1[:, :], ONE1[:, :])
        v.tensor_copy(BC[:, 0:4], BCR[:, 0:4])
        sVE.inc(v.tensor_copy(TMP1[:, :], ONE1[:, :]))
    scw_done = sVE.c

    @B.scalar
    def _(s):
        s.wait_ge(sCON.h, consts_ld)
        s.activation(ALS[:, :], LSR[:, :], AF.Exp, scale=1.0)
        sSE.inc(s.copy(TMPS[:, :], ONE1[:, :]))
    als_done = sSE.c

    @B.vector
    def _(v):
        v.wait_ge(sSE.h, als_done)
        v.tensor_scalar_mul(ALS[:, :], ALS[:, :], ISQ32)
        sVE.inc(v.tensor_copy(TMP1[:, :], ONE1[:, :]))
    als2_done = sVE.c

    @B.tensor
    def _(te):
        te.wait_ge(sVE.h, max(scw_done, als2_done))
        ins = te.matmul(PF[4][0:128, 0:16], ONEROW[:, :], ALS[:, :],
                        start=True, stop=True)
        sTE.inc(ins)
    bc_mm = sTE.c

    @B.vector
    def _(v):
        v.wait_ge(sTE.h, bc_mm)
        v.tensor_copy(AB[:, :], PF[4][0:128, 0:16])
        sVE.inc(v.tensor_copy(TMP1[:, :], ONE1[:, :]))
    bc_done = sVE.c
    ru, rg, rd = BC[:, 0:1], BC[:, 1:2], BC[:, 2:3]
    su, sg, sd = BC[:, 4:5], BC[:, 5:6], BC[:, 6:7]

    def emit_quant(mname, apw, rvec, first_dma_waits, first_comp_waits):
        qdma = []
        qcomp = []
        qout = []
        for pre in range(2):
            @B.scalar
            def _(s, pre=pre):
                for (sh, val) in (first_dma_waits if pre == 0 else []):
                    s.wait_ge(sh.h, val)
                sQI[pre].inc(s.dma_start(out=WST[:, 512 * pre:512 * (pre + 1)],
                                    in_=apw[128 * pre:128 * (pre + 1), :]), 16)
            qdma.append(sQI[pre].c)

        for ch in range(NQCH):
            slot = ch % 2

            @B.vector
            def _(v, ch=ch, slot=slot):
                v.wait_ge(sQI[slot].h, qdma[ch])
                if ch == 0:
                    for (sh, val) in first_comp_waits:
                        v.wait_ge(sh.h, val)
                if ch >= 2:
                    v.wait_ge(sQO[slot].h, qout[ch - 2])
                wsl = WST[:, 512 * slot:512 * (slot + 1)]
                v.tensor_scalar(wsl, wsl, rvec, MAGIC, ALU.mult, ALU.add)
                v.tensor_copy(TMP1[:, :], ONE1[:, :])
                v.tensor_scalar(wsl, wsl, MAGIC, -1.0, ALU.subtract, ALU.max)
                v.tensor_copy(TMP1[:, :], ONE1[:, :])
                v.tensor_scalar_min(qbf(slot), wsl, 1.0)
                sVE.inc(v.tensor_copy(TMP1[:, :], ONE1[:, :]))
            qcomp.append(sVE.c)

            @B.scalar
            def _(s, ch=ch, slot=slot):
                s.wait_ge(sVE.h, qcomp[ch])
                sQO[slot].inc(s.dma_start(out=q_in[mname].ap()[128 * ch:128 * (ch + 1), :],
                                    in_=qbf(slot)), 16)
                if ch + 2 < NQCH:
                    sQI[slot].inc(s.dma_start(
                        out=WST[:, 512 * slot:512 * (slot + 1)],
                        in_=apw[128 * (ch + 2):128 * (ch + 3), :]), 16)
            qout.append(sQO[slot].c)
            if ch + 2 < NQCH:
                qdma.append(sQI[slot].c)

        return qcomp[-1], (sQO[0].c, sQO[1].c)

    def emit_qcc(mname, qo_counts):
        @B.gpsimd
        def _(gp):
            gp.wait_ge(sQO[0].h, qo_counts[0])
            gp.wait_ge(sQO[1].h, qo_counts[1])
            sCC.inc(gp.collective_compute(
                "AllGather", ALU.bypass, replica_groups=RG,
                ins=[q_in[mname].ap()], outs=[q_out[mname].ap()]))
        return sCC.c

    qc_up, qo_up = emit_quant(
        "up", ap_wup, ru,
        first_dma_waits=[(sSE, sc_abs[-1])], first_comp_waits=[(sVE, bc_done)])
    qc_gate, qo_gate = emit_quant(
        "gate", ap_wgate, rg,
        first_dma_waits=[(sVE, qc_up)], first_comp_waits=[(sQO[0], qo_up[0]), (sQO[1], qo_up[1])])

    if _PHASE_LIMIT == 3:
        return _close(_finish(nc, B, tiles, tap_aps, locals()), ctx)

    @B.sync
    def _(s):
        s.wait_ge(sCC.h, cc_kv)
        for c in range(8):
            sG.inc(s.dma_start(out=gwt(c),
                               in_=kv_out.ap()[142 * c + 14:142 * c + 142, :]), 16)
        for c in range(8):
            sG.inc(s.dma_start(out=GPT[14 * c:14 * (c + 1), :],
                               in_=kv_out.ap()[142 * c:142 * c + 14, :]), 16)
    g_ld = sG.c

    JD16 = JMAT[:, 144:272]
    JD112 = JMAT[:, 16:144]

    kvasm = []
    for k in range(9):
        wk = 128 if k < 8 else 112

        @B.tensor
        def _(te, k=k):
            if k == 0:
                te.wait_ge(sG.h, g_ld)
            if kvasm:
                te.wait_ge(sVE.h, kvasm[-1])
            if k == 0:
                ins = te.matmul(PF[3][0:128, 0:256], PGP[0:112, 0:128],
                          GPT[0:112, 256:512], start=True, stop=False)
                ins = te.matmul(PF[3][0:128, 0:256], JD112[:, :],
                          gwt(0)[:, 256:512], start=False, stop=True)
            elif k < 8:
                ins = te.matmul(PF[3][0:128, 0:256], JD16[:, :],
                          gwt(k - 1)[:, 256:512], start=True, stop=False)
                ins = te.matmul(PF[3][0:128, 0:256], JD112[:, :],
                          gwt(k)[:, 256:512], start=False, stop=True)
            else:
                ins = te.matmul(PF[3][0:128, 0:256], JD16[:, :],
                          gwt(7)[:, 256:512], start=True, stop=True)
            sTE.inc(ins)
        vmm = sTE.c

        @B.vector
        def _(v, k=k):
            v.wait_ge(sTE.h, vmm)
            v.tensor_copy(VTS[:, 256 * k:256 * (k + 1)], PF[3][0:128, 0:256])
            sVE.inc(v.tensor_copy(TMP1[:, :], ONE1[:, :]))
        kvasm.append(sVE.c)

    for k in range(9):
        wk = 128 if k < 8 else 112
        for dh in range(2):
            @B.tensor
            def _(te, k=k, dh=dh):
                te.wait_ge(sVE.h, kvasm[-1])
                if k == 0:
                    ins = te.matmul(PF[4][0:128, 0:128],
                              GPT[0:112, 128 * dh:128 * (dh + 1)],
                              PGP[0:112, 0:128], start=True, stop=False)
                    ins = te.matmul(PF[4][0:128, 0:128],
                              gwt(0)[:, 128 * dh:128 * (dh + 1)],
                              JD112[:, :], start=False, stop=True)
                elif k < 8:
                    ins = te.matmul(PF[4][0:128, 0:128],
                              gwt(k - 1)[:, 128 * dh:128 * (dh + 1)],
                              JD16[:, :], start=True, stop=False)
                    ins = te.matmul(PF[4][0:128, 0:128],
                              gwt(k)[:, 128 * dh:128 * (dh + 1)],
                              JD112[:, :], start=False, stop=True)
                else:
                    ins = te.matmul(PF[4][0:128, 0:112],
                              gwt(7)[:, 128 * dh:128 * (dh + 1)],
                              JD16[:, 0:112], start=True, stop=True)
                sTE.inc(ins)
            kmm = sTE.c

            @B.vector
            def _(v, k=k, dh=dh, wk=wk):
                v.wait_ge(sTE.h, kmm)
                v.tensor_copy(KCH[2 * dh][:, 128 * k:128 * k + wk], PF[4][0:64, 0:wk])
                v.tensor_copy(KCH[2 * dh + 1][:, 128 * k:128 * k + wk],
                                      PF[4][64:128, 0:wk])
                sVE.inc(v.tensor_copy(TMP1[:, :], ONE1[:, :]))
            kvasm.append(sVE.c)

    @B.tensor
    def _(te):
        te.wait_ge(sVE.h, kvasm[-1])
        for vh in range(2):
            for k in range(9):
                wk = 128 if k < 8 else 112
                ins = te.matmul(PF[2 + 2 * vh][0:128, 0:1],
                          VTS[0:wk, 256 * k + 128 * vh:256 * k + 128 * (vh + 1)],
                          V136[0:wk, :], start=(k == 0), stop=(k == 8))
        sTE.inc(ins)
    mv_mm = sTE.c

    @B.vector
    def _(v):
        v.wait_ge(sTE.h, mv_mm)
        v.tensor_copy(MV[:, 0:1], PF[2][0:128, 0:1])
        v.tensor_copy(MV[:, 1:2], PF[4][0:128, 0:1])
        sVE.inc(v.tensor_copy(TMP1[:, :], ONE1[:, :]))
    mv_evac = sVE.c

    @B.tensor
    def _(te):
        te.wait_ge(sVE.h, mv_evac)
        ins = te.matmul(PF[5][0:1, 0:512], MV[:, 0:1], RDUP[:, :], start=True, stop=True)
        ins = te.matmul(PF[6][0:1, 0:512], MV[:, 1:2], RDUP[:, :], start=True, stop=True)
        sTE.inc(ins)
    mvt_mm = sTE.c

    @B.vector
    def _(v):
        v.wait_ge(sTE.h, mvt_mm)
        v.tensor_copy(MVD[0:1, 0:512], PF[5][0:1, 0:512])
        v.tensor_copy(MVD[0:1, 512:1024], PF[6][0:1, 0:512])
        sVE.inc(v.tensor_copy(TMP1[:, :], ONE1[:, :]))
    mvr_done = sVE.c

    tiles["vts"] = VTS
    for _k in range(4):
        tiles[f"kch{_k}"] = KCH[_k]
    if _PHASE_LIMIT == 4:
        return _close(_finish(nc, B, tiles, tap_aps, locals()), ctx)

    wq_ld = {}
    wq_ld[0] = emit_wslab(ap_wq[0:ND * 128, :], 0, [(sVE, kv_evac[3])])
    wq_ld[1] = emit_wslab(ap_wq[ND * 128:2 * ND * 128, :], 1, [(sTE, kv_mm[3])])

    last_pv_evac = None
    last_stt = None
    last_pt_evac = None
    for g in range(8):
        slot = g % 2
        kvh = g // 2
        vh = kvh // 2
        kvl = kvh % 2

        @B.tensor
        def _(te, g=g, slot=slot):
            te.wait_ge(wq_ld[g][0].h, wq_ld[g][1])
            if g == 0:
                te.wait_ge(sVE.h, mvr_done)
            if last_pv_evac is not None:
                te.wait_ge(sVE.h, last_pv_evac)
            for th in range(2):
                for dc in range(ND):
                    ins = te.matmul(PF[th][:, :],
                              qwslab(slot)[:, 128 * dc:128 * (dc + 1)],
                              LF[:, 1024 * dc + 512 * th:1024 * dc + 512 * (th + 1)],
                              start=(dc == 0), stop=(dc == ND - 1))
            sTE.inc(ins)
        qmm_done = sTE.c

        @B.vector
        def _(v, slot=slot):
            v.wait_ge(sTE.h, qmm_done)
            for th in range(2):
                for h2 in range(2):
                    ins = v.tensor_copy(
                        QHS[2 * slot + h2][:, 512 * th:512 * (th + 1)],
                        PF[th][64 * h2:64 * (h2 + 1), :])
            ins
            sVE.inc(v.tensor_copy(TMP1[:, :], ONE1[:, :]))
        qevac_done = sVE.c

        if g < 6:
            wq_ld[g + 2] = emit_wslab(
                ap_wq[(g + 2) * ND * 128:(g + 3) * ND * 128, :], slot,
                [(sTE, qmm_done)])
        if g == 1:
            cc_up = emit_qcc("up", qo_up)
        if g == 3:
            cc_gate = emit_qcc("gate", qo_gate)

        rope_done = qevac_done
        for h2 in range(2):
            for th in range(2):
                @B.tensor
                def _(te, h2=h2, th=th, slot=slot, rope_done=rope_done):
                    te.wait_ge(sVE.h, rope_done)
                    ins = te.matmul(PF[2][0:64, :], PPERM[0:64, 0:64],
                              QHS[2 * slot + h2][:, 512 * th:512 * (th + 1)],
                              start=True, stop=True)
                    sTE.inc(ins)
                rotq = sTE.c

                @B.vector
                def _(v, h2=h2, th=th, slot=slot):
                    v.wait_ge(sTE.h, rotq)
                    q = QHS[2 * slot + h2]
                    sl = slice(512 * th, 512 * (th + 1))
                    v.tensor_tensor(q[:, sl], q[:, sl], COS[0:64, sl], ALU.mult)
                    v.tensor_tensor(PRB[0:64, 0:512], PF[2][0:64, :],
                                    SIN[0:64, sl], ALU.mult)
                    v.tensor_copy(TMP1[:, :], ONE1[:, :])
                    v.tensor_tensor(q[:, sl], q[:, sl],
                                            PRB[0:64, 0:512], ALU.add)
                    sVE.inc(v.tensor_copy(TMP1[:, :], ONE1[:, :]))
                rope_done = sVE.c

        for j in range(NT):
            kw = KWJ[j]
            nks = _cdiv(kw, 512)
            nkc = _cdiv(kw, 128)
            pv_mm = None
            for h2 in range(2):
                h = 2 * g + h2

                @B.tensor
                def _(te, j=j, h2=h2, slot=slot, kvh=kvh, kw=kw, nks=nks,
                      rope_done=rope_done, last_stt=last_stt):
                    te.wait_ge(sVE.h, rope_done)
                    if last_stt is not None:
                        te.wait_ge(sVE.h, last_stt)
                    for ks in range(nks):
                        w = min(512, kw - 512 * ks)
                        ins = te.matmul(PF[ks][0:128, 0:w],
                                  QHS[2 * slot + h2][:, 128 * j:128 * (j + 1)],
                                  KCH[kvh][:, 512 * ks:512 * ks + w],
                                  start=True, stop=False)
                        ins = te.matmul(PF[ks][0:128, 0:w], IDENT[:, :],
                                  MASK[:, MOFF[j] + 512 * ks:MOFF[j] + 512 * ks + w],
                                  start=False, stop=True)
                    sTE.inc(ins)
                sc_mm = sTE.c

                @B.vector
                def _(v, h=h, kw=kw, nks=nks):
                    v.wait_ge(sTE.h, sc_mm)
                    for ks in range(nks):
                        w = min(512, kw - 512 * ks)
                        v.tensor_reduce(NM[:, ks:ks + 1], PF[ks][0:128, 0:w],
                                        AX.X, ALU.max)
                    v.tensor_copy(TMP1[:, :], ONE1[:, :])
                    for ks in range(1, nks):
                        v.tensor_tensor(NM[:, 0:1], NM[:, 0:1], NM[:, ks:ks + 1],
                                        ALU.max)
                        v.tensor_copy(TMP1[:, :], ONE1[:, :])
                    v.tensor_scalar(NMB[:, :], NM[:, 0:1], AB[:, h:h + 1],
                                            -1.0, ALU.mult, ALU.mult)
                    sVE.inc(v.tensor_copy(TMP1[:, :], ONE1[:, :]))
                bias_done = sVE.c

                @B.scalar
                def _(s, h=h, kw=kw, nks=nks):
                    s.wait_ge(sVE.h, bias_done)
                    for ks in range(nks):
                        w = min(512, kw - 512 * ks)
                        ins = s.activation(PRB[:, 512 * ks:512 * ks + w],
                                           PF[ks][0:128, 0:w], AF.Exp,
                                           bias=NMB[:, :], scale=AB[:, h:h + 1],
                                           accum_out=SEP[:, ks:ks + 1])
                    ins
                    sSE.inc(s.copy(TMPS[:, :], ONE1[:, :]))
                exp_done = sSE.c

                @B.vector
                def _(v, j=j, kw=kw, nks=nks):
                    v.wait_ge(sSE.h, exp_done)
                    for ks in range(1, nks):
                        v.tensor_tensor(SEP[:, 0:1], SEP[:, 0:1], SEP[:, ks:ks + 1],
                                        ALU.add)
                        v.tensor_copy(TMP1[:, :], ONE1[:, :])
                    if j == 0:
                        v.tensor_tensor(SEP[:, 0:1], SEP[:, 0:1], FL63[:, :], ALU.add)
                    if j == 0 or nks > 1:
                        v.tensor_copy(TMP1[:, :], ONE1[:, :])
                    v.reciprocal(RC[:, :], SEP[:, 0:1])
                    v.tensor_copy(TMP1[:, :], ONE1[:, :])
                    v.tensor_scalar_mul(PRB[:, 0:kw], PRB[:, 0:kw],
                                                RC[:, :])
                    sVE.inc(v.tensor_copy(TMP1[:, :], ONE1[:, :]))
                stt_done = sVE.c
                last_stt = stt_done

                TRB = [0, 1, 2, 4, 5, 6]
                nbat = _cdiv(nkc, 6)
                for bat in range(nbat):
                    kcs = list(range(6 * bat, min(nkc, 6 * bat + 6)))

                    @B.tensor
                    def _(te, kw=kw, kcs=kcs, bat=bat,
                          last_pt_evac=last_pt_evac):
                        te.wait_ge(sVE.h, stt_done)
                        if last_pt_evac is not None:
                            te.wait_ge(sVE.h, last_pt_evac)
                        for n, kc in enumerate(kcs):
                            wc = min(128, kw - 128 * kc)
                            ins = te.matmul(PF[TRB[n]][0:wc, 0:128],
                                      PRB[:, 128 * kc:128 * kc + wc],
                                      IDENT[:, :], start=True, stop=True)
                        sTE.inc(ins)
                    tr_mm = sTE.c

                    @B.vector
                    def _(v, kw=kw, kcs=kcs):
                        v.wait_ge(sTE.h, tr_mm)
                        for n, kc in enumerate(kcs):
                            wc = min(128, kw - 128 * kc)
                            ins = v.tensor_copy(PT[0:wc, 128 * kc:128 * (kc + 1)],
                                                PF[TRB[n]][0:wc, 0:128])
                        ins
                        sVE.inc(v.tensor_copy(TMP1[:, :], ONE1[:, :]))
                    last_pt_evac = sVE.c
                pt_evac = last_pt_evac

                @B.tensor
                def _(te, h2=h2, kvh=kvh, kw=kw, nkc=nkc, pt_evac=pt_evac):
                    te.wait_ge(sVE.h, pt_evac)
                    pvb = PF[3] if h2 == 0 else PF[7]
                    for kc in range(nkc):
                        wc = min(128, kw - 128 * kc)
                        ins = te.matmul(pvb[64 * h2:64 * (h2 + 1), 0:128],
                                  VTS[0:wc, 256 * kc + 64 * kvh:
                                      256 * kc + 64 * kvh + 64],
                                  PT[0:wc, 128 * kc:128 * (kc + 1)],
                                  start=(kc == 0), stop=(kc == nkc - 1))
                    sTE.inc(ins)
                pv_mm = sTE.c

            @B.vector
            def _(v, g=g, j=j, pv_mm=pv_mm):
                v.wait_ge(sTE.h, pv_mm)
                v.tensor_copy(aft(g)[0:64, 128 * j:128 * (j + 1)],
                              PF[3][0:64, 0:128])
                v.tensor_copy(aft(g)[64:128, 128 * j:128 * (j + 1)],
                                      PF[7][64:128, 0:128])
                sVE.inc(v.tensor_copy(TMP1[:, :], ONE1[:, :]))
            pv_evac = sVE.c
            last_pv_evac = pv_evac

            if j == 0:
                @B.tensor
                def _(te, g=g):
                    te.wait_ge(sVE.h, pv_evac)
                    ins = te.matmul(PF[6][0:128, 0:128], MVD[0:1, 128 * g:128 * (g + 1)],
                              F01[0:1, :], start=True, stop=True)
                    sTE.inc(ins)
                fix_mm = sTE.c

                @B.vector
                def _(v, g=g):
                    v.wait_ge(sTE.h, fix_mm)
                    v.tensor_tensor(aft(g)[:, 0:128], aft(g)[:, 0:128], NF[:, :],
                                    ALU.mult)
                    v.tensor_copy(TMP1[:, :], ONE1[:, :])
                    v.tensor_tensor(aft(g)[:, 0:128], aft(g)[:, 0:128],
                                            PF[6][0:128, 0:128], ALU.add)
                    sVE.inc(v.tensor_copy(TMP1[:, :], ONE1[:, :]))
                last_pv_evac = sVE.c

    attn_done = last_pv_evac
    tiles["arena"] = ARENA
    tiles["prb"] = PRB
    tiles["pt"] = PT
    if _PHASE_LIMIT == 5:
        return _close(_finish(nc, B, tiles, tap_aps, locals()), ctx)

    wo_ld = []

    @B.gpsimd
    def _(gp):
        sWOD.inc(gp.dma_start(
            out=WOS.rearrange("p (g n) -> p g n", g=8, n=512),
            in_=ap_wo[0:1024, :].rearrange("(g p) n -> p g n", g=8, p=128)), 16)
    wo_ld.append(sWOD.c)

    if _PHASE_LIMIT == 55:
        return _close(_finish(nc, B, tiles, tap_aps, locals()), ctx)
    wo_res = []
    womm = None
    for nb in range(4):
        for t in range(NT):
            pfw = PF[5 if (nb * NT + t) % 2 else 0]

            @B.tensor
            def _(te, nb=nb, t=t, pfw=pfw):
                te.wait_ge(sWOD.h, wo_ld[nb])
                if nb == 0 and t == 0:
                    te.wait_ge(sVE.h, attn_done)
                if len(wo_res) >= 2:
                    te.wait_ge(sVE.h, wo_res[-2])
                for g in range(8):
                    ins = te.matmul(pfw[0:128, 0:512],
                              aft(g)[:, 128 * t:128 * (t + 1)],
                              WOS[:, 512 * g:512 * (g + 1)],
                              start=(g == 0), stop=(g == 7))
                sTE.inc(ins)
            womm = sTE.c

            @B.vector
            def _(v, nb=nb, t=t, pfw=pfw, womm=womm):
                v.wait_ge(sTE.h, womm)
                v.tensor_tensor(XT[t][:, 512 * nb:512 * (nb + 1)],
                                        pfw[0:128, 0:512],
                                        XT[t][:, 512 * nb:512 * (nb + 1)], ALU.add)
                sVE.inc(v.tensor_copy(TMP1[:, :], ONE1[:, :]))
            wo_res.append(sVE.c)

        if nb == 2 and _PHASE_LIMIT == 58:
            return _close(_finish(nc, B, tiles, tap_aps, locals()), ctx)
        if nb == 1 and _PHASE_LIMIT == 57:
            return _close(_finish(nc, B, tiles, tap_aps, locals()), ctx)
        if nb == 0 and _PHASE_LIMIT == 56:
            return _close(_finish(nc, B, tiles, tap_aps, locals()), ctx)
        if nb < 3:
            @B.gpsimd
            def _(gp, nb=nb, womm=womm):
                gp.wait_ge(sTE.h, womm)
                sWOD.inc(gp.dma_start(
                    out=WOS.rearrange("p (g n) -> p g n", g=8, n=512),
                    in_=ap_wo[1024 * (nb + 1):1024 * (nb + 2), :]
                    .rearrange("(g p) n -> p g n", g=8, p=128)), 16)
            wo_ld.append(sWOD.c)

    xmid_done = wo_res[-1]
    for t in range(NT):
        tiles[f"xt{t}"] = XT[t]
    if _PHASE_LIMIT == 6:
        return _close(_finish(nc, B, tiles, tap_aps, locals()), ctx)

    ln2_done = emit_ln(True, wait_v=xmid_done)
    qc_down, qo_down = emit_quant(
        "down", ap_wdown, rd,
        first_dma_waits=[(sVE, qc_gate)], first_comp_waits=[(sQO[0], qo_gate[0]), (sQO[1], qo_gate[1])])
    cc_down = emit_qcc("down", qo_down)

    ff_up_ld = {}
    ff_gate_ld = {}
    ff_dn_ld = {}

    def emit_ffn_slab(gi, mat, waits):
        slot = gi % 2
        sem = sFU[slot] if mat == "up" else sFG[slot]

        @B.sync
        def _(s):
            for (sh, val) in waits:
                s.wait_ge(sh.h, val)
            dst = fus(slot) if mat == "up" else fgs(slot)
            src = q_out[mat].ap()[512 * gi:512 * (gi + 1), :] \
                .rearrange("(dc ds) (dl e) -> (ds dl) dc e", dc=16, ds=32, dl=4, e=128)
            sem.inc(s.dma_start(
                out=dst.rearrange("p (dc e) -> p dc e", dc=ND, e=128),
                in_=src), 16)
        return (sem, sem.c)

    def emit_dn_block(b, waits):
        @B.sync
        def _(s, b=b):
            for (sh, val) in waits:
                s.wait_ge(sh.h, val)
            for i in range(EBLK):
                ec = EBLK * b + i
                sFD.inc(s.dma_start(
                    out=fdsl(i),
                    in_=q_out["down"].ap()[512 * ec:512 * (ec + 1), :]
                    .rearrange("(p a) c -> p (a c)", p=128, a=4)), 16)
        return sFD.c

    ff_up_ld[0] = emit_ffn_slab(0, "up", [(sCC, cc_up), (sVE, xmid_done)])
    ff_gate_ld[0] = emit_ffn_slab(0, "gate", [(sCC, cc_gate)])
    ff_up_ld[1] = emit_ffn_slab(1, "up", [])
    ff_gate_ld[1] = emit_ffn_slab(1, "gate", [])
    ff_dn_ld[0] = emit_dn_block(0, [(sCC, cc_down)])

    hid_stt = {}
    dn_res = []
    dn_mm_last = {}

    for b in range(NEB):
        for i in range(EBLK):
            g = EBLK * b + i
            for kind, pfi in (("up", 0), ("gate", 2)):
                for th in range(2):
                    @B.tensor
                    def _(te, g=g, kind=kind, pfi=pfi, th=th):
                        _sl = (ff_up_ld if kind == "up" else ff_gate_ld)[g]
                        te.wait_ge(_sl[0].h, _sl[1])
                        if g == 0 and kind == "up" and th == 0:
                            te.wait_ge(sVE.h, ln2_done)
                        if (g - 1, kind, th) in hid_stt:
                            te.wait_ge(sVE.h, hid_stt[(g - 1, kind, th)])
                        slab = fus(g % 2) if kind == "up" else fgs(g % 2)
                        for dc in range(ND):
                            ins = te.matmul(
                                PF[pfi + th][0:128, 0:512],
                                slab[:, 128 * dc:128 * (dc + 1)],
                                LF[:, 1024 * dc + 512 * th:
                                   1024 * dc + 512 * (th + 1)],
                                start=(dc == 0), stop=(dc == ND - 1))
                        sTE.inc(ins)
                    hid_stt[(g, kind, th, "mm")] = sTE.c

            @B.scalar
            def _(s, g=g):
                for th in range(2):
                    s.wait_ge(sTE.h, hid_stt[(g, "up", th, "mm")])
                    ins = s.activation(PRB[:, 512 * th:512 * (th + 1)],
                                       PF[th][0:128, 0:512], AF.Silu, scale=su)
                ins
                sSE.inc(s.copy(TMPS[:, :], ONE1[:, :]))
            silu_done = sSE.c

            @B.vector
            def _(v, g=g, i=i, silu_done=silu_done):
                v.wait_ge(sSE.h, silu_done)
                for th in range(2):
                    v.wait_ge(sTE.h, hid_stt[(g, "gate", th, "mm")])
                    v.scalar_tensor_tensor(
                        HB[:, 1024 * i + 512 * th:1024 * i + 512 * (th + 1)],
                        PF[2 + th][0:128, 0:512], sg,
                        PRB[:, 512 * th:512 * (th + 1)], ALU.mult, ALU.mult)
                    sVE.inc(v.tensor_copy(TMP1[:, :], ONE1[:, :]))
                    hid_stt[(g, "up", th)] = sVE.c
                    hid_stt[(g, "gate", th)] = sVE.c

            if g + 2 < EC:
                ff_up_ld[g + 2] = emit_ffn_slab(
                    g + 2, "up", [(sTE, hid_stt[(g, "up", 1, "mm")])])
                ff_gate_ld[g + 2] = emit_ffn_slab(
                    g + 2, "gate", [(sTE, hid_stt[(g, "gate", 1, "mm")])])

        hid_all = max(hid_stt[(EBLK * b + i, "gate", th)]
                      for i in range(EBLK) for th in range(2))
        for t in range(NT):
            for nb4 in range(4):
                pfd = PF[4 + (t * 4 + nb4) % 2]

                @B.tensor
                def _(te, b=b, t=t, nb4=nb4, pfd=pfd, hid_all=hid_all):
                    te.wait_ge(sVE.h, hid_all)
                    te.wait_ge(sFD.h, ff_dn_ld[b])
                    if len(dn_res) >= 2:
                        te.wait_ge(sVE.h, dn_res[-2])
                    for i in range(EBLK):
                        ins = te.matmul(pfd[0:128, 0:512],
                                  HB[:, 1024 * i + 128 * t:1024 * i + 128 * (t + 1)],
                                  fdsl(i)[:, 512 * nb4:512 * (nb4 + 1)],
                                  start=(i == 0), stop=(i == EBLK - 1))
                    sTE.inc(ins)
                dnmm = sTE.c

                @B.vector
                def _(v, t=t, nb4=nb4, pfd=pfd, dnmm=dnmm):
                    v.wait_ge(sTE.h, dnmm)
                    v.scalar_tensor_tensor(
                        XT[t][:, 512 * nb4:512 * (nb4 + 1)], pfd[0:128, 0:512],
                        sd, XT[t][:, 512 * nb4:512 * (nb4 + 1)],
                        ALU.mult, ALU.add)
                    sVE.inc(v.tensor_copy(TMP1[:, :], ONE1[:, :]))
                dn_res.append(sVE.c)
        dn_mm_last[b] = sTE.c

        if b + 1 < NEB:
            ff_dn_ld[b + 1] = emit_dn_block(b + 1, [(sTE, dn_mm_last[b])])

    final_res = dn_res[-1]

    @B.sync
    def _(s):
        s.wait_ge(sVE.h, final_res)
        for t in range(NT):
            sST.inc(s.dma_start(out=ap_out[128 * t:128 * (t + 1), :],
                                in_=XT[t][:, :]), 16)
        s.wait_ge(sST.h, sST.c)

    return _close(_finish(nc, B, tiles, tap_aps, locals()), ctx)


def _close(nc, ctx):
    for c in reversed(ctx):
        c.__exit__(None, None, None)
    return nc


def _finish(nc, B, tiles, tap_aps, env):
    sTE, sVE, sSE = env["sTE"], env["sVE"], env["sSE"]
    sST = env["sST"]
    sTAP = env["sTAP"]
    if _TAPS:
        @B.gpsimd
        def _(gp):
            gp.wait_ge(sTE.h, sTE.c)
            gp.wait_ge(sVE.h, sVE.c)
            gp.wait_ge(sSE.h, sSE.c)
            for (tname, key, tp, tcn) in _TAPS:
                src = _resolve_tap(env, tiles, key, tp, tcn)
                sTAP.inc(gp.dma_start(out=tap_aps[tname], in_=src), 16)
            gp.wait_ge(sTAP.h, sTAP.c)

    if _PHASE_LIMIT is not None:
        ap_out = env["ap_out"]
        XT = env["XT"]

        @B.sync
        def _(s):
            s.wait_ge(sTE.h, sTE.c)
            s.wait_ge(sVE.h, sVE.c)
            for t in range(NT):
                sST.inc(s.dma_start(out=ap_out[128 * t:128 * (t + 1), :],
                                    in_=XT[t][:, :]), 16)
            s.wait_ge(sST.h, sST.c)
    return nc


def _resolve_tap(env, tiles, key, tp, tcn):
    parts = key.split(":")
    name = parts[0]
    if name in tiles:
        t = tiles[name]
        c0 = int(parts[1]) if len(parts) > 1 else 0
        return t[0:tp, c0:c0 + tcn]
    objs = {"bc": "BC", "ab": "AB", "poolt": "POOLT", "pw": "PW", "pp": "PP",
            "scw": "SCW", "mv": "MV", "hb": "HB", "qbf": "QBF"}
    if name in objs:
        t = env[objs[name]]
        c0 = int(parts[1]) if len(parts) > 1 else 0
        return t[0:tp, c0:c0 + tcn]
    if name.upper() in env:
        t = env[name.upper()]
        c0 = int(parts[1]) if len(parts) > 1 else 0
        return t[0:tp, c0:c0 + tcn]
    if name.startswith("qh"):
        return env["QHS"][int(name[2:])][0:tp, 0:tcn]
    if name == "mvr0":
        return env["MVR"][0][0:tp, 0:tcn]
    if name == "mvr1":
        return env["MVR"][1][0:tp, 0:tcn]
    raise KeyError(key)



def _host_prep(inputs):
    x = np.asarray(inputs["L0_attn_x"]).reshape(T, D)
    wq_sem = np.asarray(inputs["L0_attn_Wq_sem"])
    wk_sem = np.asarray(inputs["L0_attn_Wk_sem"])
    wq_geo = np.asarray(inputs["L0_attn_Wq_geo"])
    wk_geo = np.asarray(inputs["L0_attn_Wk_geo"])
    wv = np.asarray(inputs["L0_attn_Wv"])
    wo = np.asarray(inputs["L0_attn_Wo"])
    ls = np.asarray(inputs["L0_attn_logit_scale"])
    w_up = np.asarray(inputs["L0_ffn_W_up"])
    w_gate = np.asarray(inputs["L0_ffn_W_gate"])
    w_down = np.asarray(inputs["L0_ffn_W_down"])

    wq_cat = np.empty((D, 1024), np.float32)
    for h in range(H):
        wq_cat[:, 64 * h:64 * h + 32] = wq_sem[:, 32 * h:32 * (h + 1)]
        wq_cat[:, 64 * h + 32:64 * h + 64] = wq_geo[:, 32 * h:32 * (h + 1)]
    wq_t = np.ascontiguousarray(
        wq_cat.reshape(ND, 128, 8, 128).transpose(2, 0, 1, 3)).reshape(-1, 128)

    wkv_cat = np.empty((D, 512), np.float32)
    for kv in range(HKV):
        wkv_cat[:, 64 * kv:64 * kv + 32] = wk_sem[:, 32 * kv:32 * (kv + 1)]
        wkv_cat[:, 64 * kv + 32:64 * kv + 64] = wk_geo[:, 32 * kv:32 * (kv + 1)]
    wkv_cat[:, 256:512] = wv
    wkv_t = np.ascontiguousarray(
        wkv_cat.reshape(ND, 128, 4, 128).transpose(2, 0, 1, 3)).reshape(-1, 128)

    wo_t = np.ascontiguousarray(
        wo.reshape(8, 128, 4, 512).transpose(2, 0, 1, 3)).reshape(-1, 512)

    upt = np.ascontiguousarray(
        w_up.T.reshape(ND, 128, EC, 128).transpose(2, 0, 1, 3))
    gatet = np.ascontiguousarray(
        w_gate.T.reshape(ND, 128, EC, 128).transpose(2, 0, 1, 3))
    downt = np.ascontiguousarray(w_down.T)

    ident = np.eye(128, dtype=np.float32).astype(BFD)
    pperm = np.zeros((128, 128), np.float32)
    for m in range(128):
        mp = m % 64
        if mp >= 32:
            i = mp - 32
            pperm[(m - mp) + 32 + ((i + 16) % 32), m] = 1.0
    pperm = pperm.astype(BFD)
    pgp = np.zeros((128, 128), np.float32)
    for c in range(8):
        for j in range(7):
            for s2 in range(2):
                pgp[14 * c + 2 * j + s2, 16 * j + 2 * c + s2] = 1.0
    pgp = pgp.astype(BFD)
    jmat = np.zeros((128, 384), np.float32)
    for r in range(128):
        jmat[r, r + 128] = 1.0
    jmat = jmat.astype(BFD)
    rdup = np.zeros((128, 512), np.float32)
    for n in range(512):
        rdup[64 * ((n // 128) // 2) + n % 64, n] = 1.0
    rdup = rdup.astype(BFD)

    k_pos = np.concatenate([np.arange(NBLK) * MB + (MB - 1),
                            np.arange(T - LW, T)])

    consts = dict(
        c_one1=np.ones((128, 1), np.float32),
        c_eps=np.full((128, 1), EPS_LN, np.float32),
        c_v136=np.full((128, 1), 1.0 / KEYS, np.float32).astype(BFD),
        c_bcs=np.ones((128, 8), np.float32),
        c_onerow=np.ones((1, 128), np.float32),
        c_zf64=np.zeros((128, 64), np.float32),
        c_zf4=np.zeros((128, 4), np.float32),
        c_zrow=np.zeros((1, 8), np.float32),
        c_zbf=np.zeros((128, 512), np.float32).astype(BFD),
        c_zmvd=np.zeros((1, 1024), np.float32).astype(BFD),
        c_zmv=np.zeros((128, 2), np.float32).astype(BFD),
    )
    shared = dict(wq=wq_t, wkv=wkv_t, wo=wo_t, rdup=rdup, **consts,
                  ls=np.ascontiguousarray(ls.reshape(1, 16)).astype(np.float32),
                  pperm=pperm, ident=ident, pgp=pgp, jmat=jmat)

    per_core = []
    for c in range(NCORES):
        gts = [c + 8 * j for j in range(NT)]
        xs = np.ascontiguousarray(x.reshape(64, 128, D)[gts].reshape(TLOC, D))

        mask = np.zeros((128, MTOT), np.float32)
        for j in range(NT):
            kw = KWJ[j]
            q = 128 * gts[j] + np.arange(128)[:, None]
            mask[:, MOFF[j]:MOFF[j] + kw] = np.where(
                k_pos[None, :kw] <= q, 0.0, -30000.0)
        mask = mask.astype(BFD)

        pos = np.concatenate([128 * gt + np.arange(128) for gt in gts]
                             ).astype(np.float64)
        cosq = np.ones((128, TLOC), np.float32)
        sinq = np.zeros((128, TLOC), np.float32)
        for p in range(128):
            pat = p % 64
            if pat >= 32:
                i = pat - 32
                inv = 10000.0 ** (-(2.0 * (i % 16)) / 32.0)
                ang = pos * inv
                cosq[p, :] = np.cos(ang).astype(np.float32)
                sv = np.sin(ang).astype(np.float32)
                sinq[p, :] = -sv if i < 16 else sv
        cosq = cosq.astype(BFD)
        sinq = sinq.astype(BFD)

        f01 = np.zeros((1, 128), np.float32)
        fl63 = np.zeros((128, 1), np.float32)
        nf = np.ones((128, 128), np.float32)
        if c == 0:
            f01[0, 0:63] = 1.0
            fl63[0:63, 0] = 1.0
            nf[:, 0:63] = 0.0
        per_core.append(dict(
            x=xs, mask=mask, cosq=cosq, sinq=sinq,
            f01=f01.astype(BFD), nf=nf.astype(BFD),
            fl63=np.ascontiguousarray(fl63),
            wup=np.ascontiguousarray(upt[8 * c:8 * (c + 1)]).reshape(4096, 512),
            wgate=np.ascontiguousarray(gatet[8 * c:8 * (c + 1)]).reshape(4096, 512),
            wdown=np.ascontiguousarray(
                downt[1024 * c:1024 * (c + 1)]).reshape(4096, 512),
        ))
    return shared, per_core


_TAP_NP = None


def kernel(**inputs):
    shared, per_core = _host_prep(inputs)
    nc = _build()
    in_maps = []
    for c in range(NCORES):
        m = dict(shared)
        m.update(per_core[c])
        in_maps.append(m)
    res = run_bass_kernel_spmd(nc, in_maps, core_ids=list(range(NCORES)))
    if _TAP_NP is not None:
        _TAP_NP.clear()
        _TAP_NP.extend(res.results)
    out = np.empty((64, 128, D), np.float32)
    for c in range(NCORES):
        gts = [c + 8 * j for j in range(NT)]
        out[gts] = np.asarray(res.results[c]["xout"]).reshape(NT, 128, D)
    return out.reshape(1, T, D)
